# version 11 (final)
# baseline (speedup 1.0000x reference)
import numpy as np
import ml_dtypes
import concourse.bass as bass
import concourse.mybir as mybir
from concourse.bass_utils import run_bass_kernel_spmd

F32 = mybir.dt.float32
BF16 = mybir.dt.bfloat16
AF = mybir.ActivationFunctionType
ALU = mybir.AluOpType
AX = mybir.AxisListType


def f(v):
    return np.float32(v)


NSEQ, T, D = 4, 2048, 1024
NT = 16
NDC = 8
E = 8
NHC = 256
G = 8
NG = NHC // G
NH, HD = 16, 64
EPS_LN, EPS_RMS = 1e-5, 1e-6
BIG = 1e30

ENGS = ("sync", "tensor", "vector", "scalar", "gpsimd")


class Prog:
    def __init__(self, nc):
        self.nc = nc
        self.ops = {e: [] for e in ENGS}
        self.sems = {}
        self.handles = {}

    def sem(self, *names):
        for n in names:
            self.handles[n] = self.nc.alloc_semaphore(n)
            self.sems[n] = 0

    def op(self, eng, fn, waits=(), inc=None, inc_k=1):
        n = None
        if inc is not None:
            self.sems[inc] += inc_k
            n = self.sems[inc]
        self.ops[eng].append((fn, tuple((s, c) for (s, c) in waits if c > 0), inc, inc_k))
        return n

    def cnt(self, s):
        return self.sems[s]

    def barrier(self):
        self.sems["ph"] += 5
        tgt = self.sems["ph"]
        h = self.handles["ph"]
        for eng in ENGS:
            self.ops[eng].append((lambda e, h=h: e.sem_inc(h, 1), (), None, 0))
            self.ops[eng].append(("WAIT_ONLY", (("ph", tgt),), None, 0))

    def simulate(self):
        q = {e: list(self.ops[e]) for e in ENGS}
        idx = {e: 0 for e in ENGS}
        val = {s: 0 for s in self.sems}
        raise NotImplementedError

    def replay(self):
        nc = self.nc
        with nc.Block() as blk:
            def mk(eng_name):
                def body(eng):
                    low = {}
                    for fn, waits, inc, inc_k in self.ops[eng_name]:
                        for (sname, cntv) in waits:
                            if cntv > low.get(sname, 0):
                                eng.wait_ge(self.handles[sname], cntv)
                                low[sname] = cntv
                        if fn == "WAIT_ONLY":
                            continue
                        b = fn(eng)
                        if inc is not None:
                            b.then_inc(self.handles[inc], inc_k)
                return body
            blk.sync(mk("sync"))
            blk.tensor(mk("tensor"))
            blk.vector(mk("vector"))
            blk.scalar(mk("scalar"))
            blk.gpsimd(mk("gpsimd"))


def simulate_schedule(P, barrier_marks):
    idx = {e: 0 for e in ENGS}
    val = {s: 0 for s in P.sems}
    total = sum(len(P.ops[e]) for e in ENGS)
    done = 0
    progress = True
    while progress:
        progress = False
        for e in ENGS:
            while idx[e] < len(P.ops[e]):
                fn, waits, inc, inc_k = P.ops[e][idx[e]]
                if any(val[s] < c for (s, c) in waits):
                    break
                if (e, idx[e]) in barrier_marks:
                    val["ph"] += 1
                if inc is not None:
                    val[inc] += inc_k
                idx[e] += 1
                done += 1
                progress = True
    if done != total:
        detail = []
        for e in ENGS:
            if idx[e] < len(P.ops[e]):
                fn, waits, inc, inc_k = P.ops[e][idx[e]]
                blocked = [(s, c, val[s]) for (s, c) in waits if val[s] < c]
                detail.append(f"{e}@{idx[e]}/{len(P.ops[e])} blocked on {blocked}")
        raise RuntimeError("schedule deadlock: " + "; ".join(detail))


class NS:
    pass


def build_nc():
    nc = bass.Bass("TRN2", target_bir_lowering=False)
    B = NS()

    B.x_d = nc.dram_tensor("x", [NSEQ, T, D], F32, kind="ExternalInput").ap()
    B.w1r_d = nc.dram_tensor("w1r", [128, NHC, NDC, 128], F32, kind="ExternalInput").ap()
    B.w2r_d = nc.dram_tensor("w2r", [128, NHC, D], F32, kind="ExternalInput").ap()
    B.wqt_d = nc.dram_tensor("wqt", [128, 8, 8, 128], F32, kind="ExternalInput").ap()
    B.wkt_d = nc.dram_tensor("wkt", [128, 8, 8, 128], F32, kind="ExternalInput").ap()
    B.wvt_d = nc.dram_tensor("wvt", [128, 8, D], F32, kind="ExternalInput").ap()
    B.wot_d = nc.dram_tensor("wot", [128, 8, D], F32, kind="ExternalInput").ap()
    B.wgr_d = nc.dram_tensor("wgr", [128, 8, 8], F32, kind="ExternalInput").ap()
    B.b1c_d = nc.dram_tensor("b1c", [128, NHC], F32, kind="ExternalInput").ap()
    B.b2_d = nc.dram_tensor("b2", [8, D], F32, kind="ExternalInput").ap()
    B.bg_d = nc.dram_tensor("bg", [1, 8], F32, kind="ExternalInput").ap()
    B.bq_d = nc.dram_tensor("bq", [1, D], F32, kind="ExternalInput").ap()
    B.bk_d = nc.dram_tensor("bk", [1, D], F32, kind="ExternalInput").ap()
    B.bv_d = nc.dram_tensor("bv", [1, D], F32, kind="ExternalInput").ap()
    B.bo_d = nc.dram_tensor("bo", [1, D], F32, kind="ExternalInput").ap()
    B.nqw_d = nc.dram_tensor("nqw", [128, 8], F32, kind="ExternalInput").ap()
    B.nkw_d = nc.dram_tensor("nkw", [128, 8], F32, kind="ExternalInput").ap()
    B.y_d = nc.dram_tensor("y", [NSEQ, T, D], F32, kind="ExternalOutput").ap()

    idB_np = np.eye(128, dtype=ml_dtypes.bfloat16)
    idF_np = np.eye(128, dtype=np.float32)
    selE_np = np.zeros((8, 8 * 128), dtype=ml_dtypes.bfloat16)
    for e_ in range(8):
        selE_np[e_, e_ * 128:(e_ + 1) * 128] = 1
    B.idB_d = nc.inline_tensor(idB_np, "idB").ap()
    B.idF_d = nc.inline_tensor(idF_np, "idF").ap()
    B.oK1_d = nc.inline_tensor(np.ones((1, 128), dtype=ml_dtypes.bfloat16), "oK1c").ap()
    B.oP_d = nc.inline_tensor(np.ones((65, 128), dtype=ml_dtypes.bfloat16), "oPc").ap()
    B.oR_d = nc.inline_tensor(np.ones((1, 512), dtype=ml_dtypes.bfloat16), "oRc").ap()
    B.oC_d = nc.inline_tensor(np.ones((128, 1), dtype=ml_dtypes.bfloat16), "oCc").ap()
    B.selE_d = nc.inline_tensor(selE_np, "selEc").ap()

    SB_BASE = (nc.sbuf_base + 31) & ~31
    off = [SB_BASE]

    def sb(name, shape, dt, at=None, advance=True):
        per = int(np.prod(shape[1:])) * (2 if dt == BF16 else 4)
        per = (per + 31) & ~31
        o = off[0] if at is None else at
        t = nc.alloc_sbuf_tensor_at(name, shape, dt, offset=o)
        if at is None and advance:
            off[0] = o + per
        return t

    B.A_x = sb("A_x", [128, NT * 1024], F32)
    B.idB = sb("idB_s", [128, 128], BF16)
    B.idF = sb("idF_s", [128, 128], F32)
    B.oK1 = sb("oK1_s", [1, 128], BF16)
    B.oP = sb("oP_s", [65, 128], BF16)
    B.oR = sb("oR_s", [1, 512], BF16)
    B.oC = sb("oC_s", [128, 1], BF16)
    B.selE = sb("selE_s", [8, 8 * 128], BF16)
    B.nqw = sb("nqw_s", [128, 8], F32)
    B.nkw = sb("nkw_s", [128, 8], F32)
    B.brow = sb("brow_s", [1, D], BF16)
    STATIC_END = off[0]

    U0 = STATIC_END
    LNZ = U0 + 81920
    B.sqs = sb("sqs_s", [128, 1024], BF16, at=LNZ)
    B.uscr = sb("uscr_s", [128, 1024], BF16, at=LNZ + 2048)
    B.stt = sb("stt_s", [128, 64], F32, at=LNZ + 4096)
    B.u_fm = sb("u_fm", [128, NDC, T], BF16, at=U0)
    B.w1g = sb("w1g", [128, 2, G * NDC * 128], BF16, at=U0 + 32768)
    B.w2g = sb("w2g", [128, G * 1024], BF16, at=U0 + 65536)
    B.hb = sb("hb", [128, 2, G, 512], BF16, at=U0 + 81920)
    B.gbc = sb("gbc", [128, 2, 512], BF16, at=U0 + 98304)
    B.hgel = sb("hgel", [128, 2, 512], BF16, at=U0 + 100352)
    B.b1c = sb("b1c_s", [128, NHC], F32, at=U0 + 102400)
    B.b2s = sb("b2s", [8, D], BF16, at=U0 + 103424)
    B.wgb = sb("wgb", [128, 64], BF16, at=U0 + 105472)
    B.bgb = sb("bgb", [1, 8], BF16, at=U0 + 105600)
    B.wfmb = sb("wfmb", [8, T], BF16, at=U0 + 105632)
    B.g_all = sb("g_all", [128, NT, 8], F32, at=U0 + 109728)
    B.topA = sb("topA", [128, NT, 8], F32, at=U0 + 110240)
    B.topB = sb("topB", [128, 64], F32, at=U0 + 110752)
    B.w_tm = sb("w_tm", [128, NT, 8], F32, at=U0 + 111008)
    FFN_END = U0 + 111520
    B.k_all = sb("k_all", [128, 8, T], BF16, at=U0 + 32768)
    B.v_tm = sb("v_tm", [128, 16, NH, 65], BF16, at=U0 + 65536)
    B.wslot = sb("wslot", [128, 8192], BF16, at=U0 + 98816)
    B.wslot3 = sb("wslot3", [128, 8, 1024], BF16, at=U0 + 98816)
    AM = U0 + 115200
    B.atm = sb("atm", [128, 2048], BF16, at=AM)
    B.atT = sb("atT", [128, 2048], BF16, at=AM + 17408)
    B.attnT = sb("attnT", [128, 2, 512], BF16, at=AM + 5120)
    B.apair = sb("apair", [128, 8, 512], BF16, at=AM + 7168)
    B.qpre = sb("qpre", [128, 8, 512], BF16, at=AM + 7168)
    B.aoscr = sb("aoscr", [128, 2, 512], BF16, at=AM + 15360)
    B.sqs2 = sb("sqs2", [128, 2, 512], BF16, at=AM + 15360)
    B.rbcs = sb("rbcs", [128, 512], BF16, at=AM + 17408)
    B.rr = sb("rr", [65, 512], F32, at=AM + 18432)
    B.rrb = sb("rrb", [65, 512], BF16, at=AM + 20480)
    B.ast = sb("ast", [128, 16], F32, at=AM + 21504)
    ATTN_END = AM + 22144
    LIMIT = nc.sbuf_top
    assert max(FFN_END, ATTN_END) <= LIMIT, (STATIC_END, FFN_END, ATTN_END, LIMIT)

    B.ps_h = nc.alloc_psum_tensor("ps_h", [128, 2, 512], F32)
    B.ps_y = nc.alloc_psum_tensor("ps_y", [128, 2048], F32)
    B.ps_a = nc.alloc_psum_tensor("ps_a", [128, 512], F32)
    B.ps_t = nc.alloc_psum_tensor("ps_t", [128, 1024], BF16)

    P = Prog(nc)
    P.sem("ph", "dx", "dw",
          "lna", "lnv", "lnp", "uev", "glp", "gle",
          "gbp", "gbv", "hpe", "hge", "hbv", "w2i", "ytk", "yfl", "wpe",
          "c2p", "c2v", "c2a", "c2e",
          "s1p", "s1v", "mtp", "qtv", "stp", "exa", "avp", "zbp", "zv", "aov",
          "opp", "orv", "prd")

    barrier_marks = set()

    def barrier():
        P.sems["ph"] += 5
        tgt = P.sems["ph"]
        h = P.handles["ph"]
        for eng in ENGS:
            barrier_marks.add((eng, len(P.ops[eng])))
            P.ops[eng].append((lambda e, h=h: e.sem_inc(h, 1), (), None, 0))
            P.ops[eng].append(("WAIT_ONLY", (("ph", tgt),), None, 0))

    st = NS()
    st.nh = 0
    st.it = 0
    st.ym = 0
    st.kvb = 0
    st.vt = 0
    st.aun = 0
    st.exa_prev = 0
    st.pair_dma = {}

    def emit_init():
        for (dst, src) in ((B.idB, B.idB_d), (B.idF, B.idF_d), (B.oK1, B.oK1_d),
                           (B.oP, B.oP_d), (B.oR, B.oR_d), (B.oC, B.oC_d),
                           (B.selE, B.selE_d), (B.nqw, B.nqw_d), (B.nkw, B.nkw_d)):
            P.op("sync", lambda e, d=dst, s=src: e.dma_start(d[:], s), (), "dx", 16)
        barrier()
        cnt = P.cnt("dx")
        for eng in ENGS:
            P.ops[eng].append(("WAIT_ONLY", (("dx", cnt),), None, 0))

    def emit_ffn_consts(extra_waits=()):
        w = list(extra_waits)
        P.op("gpsimd", lambda e: e.dma_start(B.b1c[:], B.b1c_d), w, "dw", 16)
        P.op("gpsimd", lambda e: e.dma_start(B.b2s[:], B.b2_d), (), "dw", 16)
        P.op("gpsimd", lambda e: e.dma_start(B.wgb[:], B.wgr_d.rearrange("p a b -> p (a b)")), (), "dw", 16)
        P.op("gpsimd", lambda e: e.dma_start(B.bgb[:], B.bg_d), (), "dw", 16)
        return P.cnt("dw")

    def emit_ln(seq, with_gates, first_waits=(), const_dw=0):
        s2d = [0] * NT
        for t in range(NT):
            xt = B.A_x[:, t * 1024:(t + 1) * 1024]
            w = list(first_waits) if t == 0 else []
            ws = w + ([("lnv", s2d[t - 1])] if t else [])
            sq = P.op("scalar", lambda e, xt=xt: e.activation(B.sqs[:], xt, AF.Square), ws, "lna")
            P.op("vector", lambda e, xt=xt, t=t: e.reduce_sum(B.stt[:, t:t + 1], xt, axis=AX.X), w)
            s2d[t] = P.op("vector", lambda e, t=t: e.reduce_sum(B.stt[:, 16 + t:17 + t], B.sqs[:], axis=AX.X),
                          [("lna", sq)], "lnv")
        v1 = P.op("vector", lambda e: e.tensor_scalar(B.stt[:, 32:48], B.stt[:, 0:16], 1.0 / D, None, ALU.mult), (), "lnv")
        v2 = P.op("vector", lambda e: e.tensor_tensor(B.stt[:, 0:16], B.stt[:, 32:48], B.stt[:, 32:48], ALU.mult),
                  [("lnv", v1)], "lnv")
        vstat = P.op("vector", lambda e: e.scalar_tensor_tensor(
            B.stt[:, 48:64], B.stt[:, 16:32], 1.0 / D, B.stt[:, 0:16], ALU.mult, ALU.subtract),
            [("lnv", v2)], "lnv")
        lnd = P.op("scalar", lambda e: e.activation(B.stt[:, 16:32], B.stt[:, 48:64], AF.Ln,
                                                    bias=f(EPS_LN), scale=f(1.0)), [("lnv", vstat)], "lna")
        rstd = P.op("scalar", lambda e: e.activation(B.stt[:, 48:64], B.stt[:, 16:32], AF.Exp,
                                                     bias=f(0.0), scale=f(-0.5)), [("lna", lnd)], "lna")
        for t in range(NT):
            xt = B.A_x[:, t * 1024:(t + 1) * 1024]
            waits = [("lna", rstd), ("lnp", P.cnt("lnp"))]
            ud = P.op("vector", lambda e, xt=xt, t=t: e.tensor_scalar(
                B.uscr[:], xt, B.stt[:, 32 + t:33 + t], B.stt[:, 48 + t:49 + t],
                ALU.subtract, ALU.mult), waits, "lnv")
            base_uev = P.cnt("uev")
            for dc in range(NDC):
                P.op("tensor", lambda e, dc=dc: e.transpose(
                    B.ps_t[:, dc * 128:(dc + 1) * 128], B.uscr[:, dc * 128:(dc + 1) * 128], B.idB[:]),
                    [("lnv", ud), ("uev", base_uev)], "lnp")
            lp = P.cnt("lnp")
            for dc in range(NDC):
                P.op("vector", lambda e, dc=dc, t=t, seq=seq: e.tensor_copy(
                    B.u_fm[:, dc, t * 128:(t + 1) * 128], B.ps_t[:, dc * 128:(dc + 1) * 128]),
                    [("lnp", lp)], "uev")
            if with_gates:
                gle0 = P.cnt("gle")
                for dc in range(NDC):
                    P.op("tensor", lambda e, dc=dc, t=t: e.matmul(
                        B.ps_a[:, 0:8], B.u_fm[:, dc, t * 128:(t + 1) * 128],
                        B.wgb[:, dc * 8:(dc + 1) * 8], start=(dc == 0), stop=False),
                        [("uev", P.cnt("uev")), ("gle", gle0), ("dw", const_dw)], "glp")
                gl = P.op("tensor", lambda e: e.matmul(
                    B.ps_a[:, 0:8], B.oK1[:], B.bgb[:], start=False, stop=True), (), "glp")
                P.op("vector", lambda e, t=t: e.tensor_copy(B.g_all[:, t, :], B.ps_a[:, 0:8]),
                     [("glp", gl)], "gle")
        if not with_gates:
            return
        def b3(ap):
            return ap.rearrange("p (t o) -> p t o", o=1).broadcast_to([128, NT, 8])
        t1 = P.op("vector", lambda e: e.reduce_max(B.topB[:, 0:16], B.g_all[:], axis=AX.X),
                  [("gle", P.cnt("gle"))], "lnv")
        t2 = P.op("vector", lambda e: e.tensor_tensor(B.topA[:], B.g_all[:], b3(B.topB[:, 0:16]), ALU.is_ge),
                  [("lnv", t1)], "lnv")
        t3 = P.op("vector", lambda e: e.scalar_tensor_tensor(B.topA[:], B.topA[:], -BIG, B.g_all[:], ALU.mult, ALU.add),
                  [("lnv", t2)], "lnv")
        t4 = P.op("vector", lambda e: e.reduce_max(B.topB[:, 16:32], B.topA[:], axis=AX.X),
                  [("lnv", t3)], "lnv")
        sub1 = P.op("vector", lambda e: e.tensor_tensor(B.topA[:], B.g_all[:], b3(B.topB[:, 0:16]), ALU.subtract),
                    [("lnv", t4)], "lnv")
        ex1 = P.op("scalar", lambda e: e.activation(B.topA[:], B.topA[:], AF.Exp), [("lnv", sub1)], "lna")
        sub2 = P.op("vector", lambda e: e.tensor_tensor(B.topB[:, 32:48], B.topB[:, 16:32], B.topB[:, 0:16],
                                                        ALU.subtract), (), "lnv")
        ex2 = P.op("scalar", lambda e: e.activation(B.topB[:, 32:48], B.topB[:, 32:48], AF.Exp),
                   [("lnv", sub2)], "lna")
        t5 = P.op("vector", lambda e: e.tensor_scalar(B.topB[:, 32:48], B.topB[:, 32:48], 1.0, None, ALU.add),
             [("lna", ex2)], "lnv")
        t6 = P.op("vector", lambda e: e.reciprocal(B.topB[:, 48:64], B.topB[:, 32:48]), [("lnv", t5)], "lnv")
        t7 = P.op("vector", lambda e: e.tensor_scalar(B.topB[:, 48:64], B.topB[:, 48:64], 0.5, None, ALU.mult),
             [("lnv", t6)], "lnv")
        t8 = P.op("vector", lambda e: e.tensor_tensor(B.g_all[:], B.g_all[:], b3(B.topB[:, 16:32]), ALU.is_ge),
             [("lnv", t7)], "lnv")
        t9 = P.op("vector", lambda e: e.tensor_tensor(B.topA[:], B.topA[:], B.g_all[:], ALU.mult),
             [("lna", ex1), ("lnv", t8)], "lnv")
        wdone = P.op("vector", lambda e: e.tensor_tensor(B.w_tm[:], B.topA[:], b3(B.topB[:, 48:64]), ALU.mult),
                     [("lnv", t9)], "lnv")
        for t in range(NT):
            tp = P.op("tensor", lambda e, t=t: e.transpose(
                B.ps_a[0:8, 128:256], B.w_tm[:, t, :], B.idF[:]),
                [("lnv", wdone), ("gle", P.cnt("gle"))], "glp")
            P.op("vector", lambda e, t=t: e.tensor_copy(B.wfmb[0:8, t * 128:(t + 1) * 128],
                                                        B.ps_a[0:8, 128:256]), [("glp", tp)], "gle")

    def emit_ffn(seq, call, const_dw):
        base_wpe = P.cnt("wpe")
        w1cnt = {}
        w2cnt = {}

        def load_group(g):
            if g >= NG:
                return
            par = g % 2
            wv = [("wpe", base_wpe + max(0, g - 1))]
            w1cnt[g] = P.op("gpsimd", lambda e, g=g, par=par: e.dma_start(
                B.w1g[:, par, :], B.w1r_d[:, g * G:(g + 1) * G, :, :].rearrange("p a b c -> p (a b c)")),
                wv, "dw", 16)
            wv2 = [("wpe", base_wpe + g)]
            w2cnt[g] = P.op("gpsimd", lambda e, g=g: e.dma_start(
                B.w2g[:], B.w2r_d[:, g * G:(g + 1) * G, :].rearrange("p a b -> p (a b)")),
                wv2, "dw", 16)

        load_group(0)
        load_group(1)
        for it_l in range(NG * 4):
            g, ts = divmod(it_l, 4)
            par = g % 2
            expert = g // 4
            it = st.it
            if ts == 0 and g + 2 < NG:
                load_group(g + 2)
            gb = P.op("tensor", lambda e, expert=expert, ts=ts: e.matmul(
                B.ps_a[:, 0:512], B.selE[:, expert * 128:(expert + 1) * 128],
                B.wfmb[0:8, ts * 512:(ts + 1) * 512], start=True, stop=True),
                [("gbv", it)], "gbp")
            P.op("vector", lambda e, it=it: e.tensor_copy(B.gbc[:, it % 2, :], B.ps_a[:, 0:512]),
                 [("gbp", gb)], "gbv")
            for hc in range(G):
                nh = st.nh
                bank = nh % 2
                for dc in range(NDC):
                    wv = []
                    if dc == 0:
                        wv = [("hge", max(0, nh - 1)), ("dw", w1cnt[g])]
                    P.op("tensor", lambda e, bank=bank, par=par, hc=hc, dc=dc, ts=ts: e.matmul(
                        B.ps_h[:, bank, :], B.w1g[:, par, (hc * NDC + dc) * 128:(hc * NDC + dc + 1) * 128],
                        B.u_fm[:, dc, ts * 512:(ts + 1) * 512],
                        start=(dc == 0), stop=(dc == NDC - 1)),
                        wv, "hpe" if dc == NDC - 1 else None)
                st.nh += 1
                nh1 = st.nh
                hcg = g * G + hc
                P.op("scalar", lambda e, bank=bank, nh1=nh1, hcg=hcg: e.activation(
                    B.hgel[:, nh1 % 2, :], B.ps_h[:, bank, :], AF.Gelu,
                    bias=B.b1c[:, hcg:hcg + 1], scale=f(1.0)),
                    [("hpe", nh1), ("hbv", max(0, (nh1 - 2) * 1))], "hge")
                P.op("vector", lambda e, it=it, hc=hc, nh1=nh1: e.tensor_tensor(
                    B.hb[:, it % 2, hc, :], B.hgel[:, nh1 % 2, :], B.gbc[:, it % 2, :], ALU.mult),
                    [("hge", nh1), ("w2i", max(0, it - 1))] + ([("gbv", it + 1)] if hc == 0 else []), "hbv")
            for tk in range(4):
                m = st.ym
                ybank = m % 2
                first = True
                for half in range(2):
                    ysl = B.ps_y[:, ybank * 1024 + half * 512: ybank * 1024 + (half + 1) * 512]
                    wv = []
                    if half == 0:
                        wv = [("hbv", st.nh), ("yfl", max(0, m - 1)), ("dw", w2cnt[g])]
                    if g == 0:
                        P.op("tensor", lambda e, ysl=ysl, ts=ts, tk=tk, half=half: e.matmul(
                            ysl, B.wfmb[0:8, ts * 512 + tk * 128: ts * 512 + (tk + 1) * 128],
                            B.b2s[:, half * 512:(half + 1) * 512], start=True, stop=False), wv)
                        wv = []
                    for hc in range(G):
                        last = (hc == G - 1 and half == 1)
                        P.op("tensor", lambda e, ysl=ysl, it=it, hc=hc, tk=tk, half=half, g=g: e.matmul(
                            ysl, B.hb[:, it % 2, hc, tk * 128:(tk + 1) * 128],
                            B.w2g[:, hc * 1024 + half * 512: hc * 1024 + (half + 1) * 512],
                            start=(hc == 0 and g != 0), stop=(hc == G - 1)),
                            wv, "ytk" if last else None)
                        wv = []
                st.ym += 1
                tt = ts * 4 + tk
                P.op("vector", lambda e, tt=tt, ybank=ybank: e.tensor_tensor(
                    B.A_x[:, tt * 1024:(tt + 1) * 1024], B.A_x[:, tt * 1024:(tt + 1) * 1024],
                    B.ps_y[:, ybank * 1024:(ybank + 1) * 1024], ALU.add),
                    [("ytk", st.ym)], "yfl")
            P.op("tensor", lambda e: e.nop(), (), "w2i")
            if ts == 3:
                P.op("tensor", lambda e: e.nop(), (), "wpe")
            st.it += 1

    def emit_qkv(seq):
        c2 = NS()
        c2.cp_hist = {0: 0, 1: 0}
        c2.ssq_hist = {0: 0, 1: 0}
        c2.rbcs_cp = 0
        wk = P.op("gpsimd", lambda e: e.dma_start(
            B.wslot[:], B.wkt_d.rearrange("p a b c -> p (a b c)")), (), "dw", 16)
        bk = P.op("gpsimd", lambda e: e.dma_start(B.brow[:], B.bk_d), (), "dw", 16)

        def qk_prod(dst, wcnt, nw):
            for ts4 in range(4):
                ssq_last = 0
                for oc in range(NDC):
                    bank = st.kvb % 2
                    st.kvb += 1
                    for ic in range(NDC):
                        wv = []
                        if ic == 0:
                            wv = [("dw", wcnt), ("c2e", c2.cp_hist[bank])]
                        P.op("tensor", lambda e, bank=bank, ic=ic, oc=oc, ts4=ts4: e.matmul(
                            B.ps_h[:, bank, :], B.wslot[:, (ic * 8 + oc) * 128:(ic * 8 + oc + 1) * 128],
                            B.u_fm[:, ic, ts4 * 512:(ts4 + 1) * 512],
                            start=(ic == 0), stop=False), wv)
                    mmq = P.op("tensor", lambda e, bank=bank, oc=oc: e.matmul(
                        B.ps_h[:, bank, :], B.brow[0:1, oc * 128:(oc + 1) * 128], B.oR[:],
                        start=False, stop=True), (), "c2p")
                    cp = P.op("vector", lambda e, oc=oc, bank=bank: e.tensor_copy(
                        B.qpre[:, oc, :], B.ps_h[:, bank, :]),
                        [("c2p", mmq), ("c2a", P.cnt("c2a"))], "c2e")
                    c2.cp_hist[bank] = cp
                    sq = P.op("scalar", lambda e, oc=oc: e.activation(
                        B.sqs2[:, oc % 2, :], B.qpre[:, oc, :], AF.Square),
                        [("c2e", cp), ("c2p", c2.ssq_hist[oc % 2])], "c2a")
                    ssq_last = P.op("tensor", lambda e, oc=oc: e.matmul(
                        B.ps_a[0:1, 0:512], B.oC[:], B.sqs2[:, oc % 2, :],
                        start=(oc == 0), stop=(oc == NDC - 1)),
                        [("c2a", sq)] + ([("c2v", c2.rbcs_cp)] if oc == 0 else []), "c2p")
                    c2.ssq_hist[oc % 2] = ssq_last
                lnc = P.op("scalar", lambda e: e.activation(B.rr[0:1, :], B.ps_a[0:1, 0:512], AF.Ln,
                                                      bias=f(EPS_RMS), scale=f(1.0 / D)),
                     [("c2p", ssq_last)], "c2a")
                ex = P.op("scalar", lambda e: e.activation(B.rr[0:1, :], B.rr[0:1, :], AF.Exp,
                                                           bias=f(0.0), scale=f(-0.5)), [("c2a", lnc)], "c2a")
                rb = P.op("vector", lambda e: e.tensor_copy(B.rrb[0:1, :], B.rr[0:1, :]),
                          [("c2a", ex)], "c2v")
                rbcmm = P.op("tensor", lambda e: e.matmul(B.ps_a[:, 0:512], B.oK1[:], B.rrb[0:1, :],
                                                          start=True, stop=True),
                             [("c2v", rb), ("c2a", ex)], "c2p")
                c2.rbcs_cp = P.op("vector", lambda e: e.tensor_copy(B.rbcs[:], B.ps_a[:, 0:512]),
                                  [("c2p", rbcmm)], "c2v")
                for oc in range(NDC):
                    P.op("vector", lambda e, oc=oc, ts4=ts4, dst=dst, nw=nw: e.scalar_tensor_tensor(
                        dst[:, oc, ts4 * 512:(ts4 + 1) * 512], B.qpre[:, oc, :], nw[:, oc:oc + 1],
                        B.rbcs[:], ALU.mult, ALU.mult),
                        [("c2v", c2.rbcs_cp)] if oc == 0 else (), "c2v")

        qk_prod(B.k_all, max(wk, bk), B.nkw)
        wvv = P.op("gpsimd", lambda e: e.dma_start(
            B.wslot[:], B.wvt_d.rearrange("p a b -> p (a b)")), [("c2p", P.cnt("c2p"))], "dw", 16)
        bvv = P.op("gpsimd", lambda e: e.dma_start(B.brow[:], B.bv_d), (), "dw", 16)
        P.op("vector", lambda e: e.memset(B.v_tm[:, :, :, 64:65], 1.0))
        vdone = [0, 0]
        for t in range(NT):
            yb = t % 2
            for half in range(2):
                ysl = B.ps_y[:, yb * 1024 + half * 512: yb * 1024 + (half + 1) * 512]
                for ic in range(NDC):
                    wv = []
                    if ic == 0 and half == 0:
                        wv = [("dw", max(wvv, bvv)), ("c2v", vdone[yb])]
                    P.op("tensor", lambda e, ysl=ysl, ic=ic, t=t, half=half: e.matmul(
                        ysl, B.u_fm[:, ic, t * 128:(t + 1) * 128],
                        B.wslot3[:, ic, half * 512:(half + 1) * 512],
                        start=(ic == 0), stop=False), wv)
                mm = P.op("tensor", lambda e, ysl=ysl, half=half: e.matmul(
                    ysl, B.oK1[:], B.brow[0:1, half * 512:(half + 1) * 512],
                    start=False, stop=True), (), "c2p")
            vdone[yb] = P.op("vector", lambda e, t=t, yb=yb: e.tensor_copy(
                B.v_tm[:, t, :, 0:64],
                B.ps_y[:, yb * 1024:(yb + 1) * 1024].rearrange("p (h w) -> p h w", w=64)),
                [("c2p", mm)], "c2v")
        wq = P.op("gpsimd", lambda e: e.dma_start(
            B.wslot[:], B.wqt_d.rearrange("p a b c -> p (a b c)")), [("c2p", P.cnt("c2p"))], "dw", 16)
        bq = P.op("gpsimd", lambda e: e.dma_start(B.brow[:], B.bq_d), (), "dw", 16)
        qk_prod(B.u_fm, max(wq, bq), B.nqw)
        P.op("gpsimd", lambda e: e.dma_start(
            B.wslot[:], B.wot_d.rearrange("p a b -> p (a b)")), [("c2p", P.cnt("c2p"))], "dw", 16)
        P.op("gpsimd", lambda e: e.dma_start(B.brow[:], B.bo_d), (), "dw", 16)
        return P.cnt("dw")

    def emit_attn(seq, wo_dw):
        for q5 in range(4):
            for h in range(NH):
                emit_unit(q5, h)
            emit_oproj(q5, wo_dw)

    def emit_bundle(bun):
        for eng in ENGS:
            for rec in bun[eng]:
                P.ops[eng].append(rec)

    def emit_merged(s1, s23):
        pe23 = s23["tensor"]
        pe1 = s1["tensor"]
        chunks = [pe1[0:4], pe1[4:8], pe1[8:12], pe1[12:16], pe1[16:20]]
        pos = [0, 4, 10, 16, 24]
        merged = []
        ci = 0
        for i, rec in enumerate(pe23):
            while ci < len(pos) and pos[ci] == i:
                merged.extend(chunks[ci]); ci += 1
            merged.append(rec)
        while ci < len(pos):
            merged.extend(chunks[ci]); ci += 1
        P.ops["tensor"].extend(merged)
        P.ops["vector"].extend(s1["vector"] + s23["vector"])
        P.ops["scalar"].extend(s23["scalar"] + s1["scalar"])
        P.ops["sync"].extend(s23["sync"] + s1["sync"])
        P.ops["gpsimd"].extend(s23["gpsimd"] + s1["gpsimd"])

    class Bundle(dict):
        pass

    def bundle_op(bun, eng, fn, waits=(), inc=None, inc_k=1):
        n = None
        if inc is not None:
            P.sems[inc] += inc_k
            n = P.sems[inc]
        bun[eng].append((fn, tuple((s, c) for (s, c) in waits if c > 0), inc, inc_k))
        return n

    def new_bundle():
        return {e: [] for e in ENGS}

    def emit_unit(q5, h):
        pr, side = h >> 1, h & 1
        q_all = B.u_fm
        for qt in range(4):
            un = st.aun
            st.aun += 1
            for kc4 in range(4):
                wv = []
                if kc4 == 0:
                    wv = [("exa", st.exa_prev), ("orv", P.cnt("orv"))]
                P.op("tensor", lambda e, side=side, pr=pr, q5=q5, qt=qt, kc4=kc4: e.matmul(
                    B.ps_y[:, kc4 * 512:(kc4 + 1) * 512],
                    q_all[side * 64:side * 64 + 64, pr, q5 * 512 + qt * 128: q5 * 512 + (qt + 1) * 128],
                    B.k_all[side * 64:side * 64 + 64, pr, kc4 * 512:(kc4 + 1) * 512],
                    start=True, stop=True), wv, "s1p")
            mx = P.op("vector", lambda e: e.reduce_max(B.ast[:, 0:1], B.ps_y[:], axis=AX.X),
                      [("s1p", P.cnt("s1p"))], "s1v")
            nm = P.op("vector", lambda e: e.tensor_scalar(B.ast[:, 1:2], B.ast[:, 0:1], -0.125, None, ALU.mult),
                      [("s1v", mx)], "s1v")
            exs = 0
            for kc4 in range(4):
                exs = P.op("scalar", lambda e, kc4=kc4: e.activation(
                    B.atm[:, kc4 * 512:(kc4 + 1) * 512], B.ps_y[:, kc4 * 512:(kc4 + 1) * 512],
                    AF.Exp, bias=B.ast[:, 1:2], scale=f(0.125)), [("s1v", nm)], "exa")
            st.exa_prev = exs
            zz = P.op("vector", lambda e: e.reduce_sum(B.ast[:, 2:3], B.atm[:], axis=AX.X),
                      [("exa", exs)], "zv")
            rz = P.op("vector", lambda e: e.reciprocal(B.ast[:, 3:4], B.ast[:, 2:3]),
                      [("zv", zz)], "zv")
            nrm = P.op("vector", lambda e: e.tensor_scalar(B.atm[:], B.atm[:], B.ast[:, 3:4], None, ALU.mult),
                       [("zv", rz)], "zv")
            for rnd in range(2):
                tps = 0
                for j in range(8):
                    kc = rnd * 8 + j
                    wv = [("zv", nrm)]
                    if j == 0:
                        wv.append(("qtv", P.cnt("qtv")))
                    tps = P.op("tensor", lambda e, kc=kc, j=j: e.transpose(
                        B.ps_t[:, j * 128:(j + 1) * 128], B.atm[:, kc * 128:(kc + 1) * 128], B.idB[:]),
                        wv, "mtp")
                for j in range(8):
                    kc = rnd * 8 + j
                    P.op("vector", lambda e, kc=kc, j=j: e.tensor_copy(
                        B.atT[:, kc * 128:(kc + 1) * 128], B.ps_t[:, j * 128:(j + 1) * 128]),
                        [("mtp", tps)], "qtv")
            par = un % 2
            for kc in range(16):
                wv = [("qtv", P.cnt("qtv"))]
                if kc == 0:
                    wv.append(("aov", max(0, P.cnt("aov") - 1)))
                P.op("tensor", lambda e, kc=kc, h=h, par=par: e.matmul(
                    B.ps_h[0:64, par, 0:128],
                    B.v_tm[:, kc, h, 0:64], B.atT[:, kc * 128:(kc + 1) * 128],
                    start=(kc == 0), stop=(kc == 15)), wv, "avp")
            if side == 0:
                P.op("vector", lambda e, pr=pr, qt=qt, par=par: e.tensor_copy(
                    B.apair[0:64, pr, qt * 128:(qt + 1) * 128], B.ps_h[0:64, par, 0:128]),
                    [("avp", P.cnt("avp"))], "aov")
            else:
                ev = P.op("vector", lambda e, pr=pr, qt=qt, par=par: e.tensor_copy(
                    B.aoscr[0:64, un % 2, 0:128], B.ps_h[0:64, par, 0:128]),
                    [("avp", P.cnt("avp")), ("dw", st.pair_dma.get(un % 2, 0))], "aov")
                st.pair_dma[un % 2] = P.op("gpsimd", lambda e, pr=pr, qt=qt: e.dma_start(
                    B.apair[64:128, pr, qt * 128:(qt + 1) * 128], B.aoscr[0:64, un % 2, 0:128]),
                    [("aov", ev)], "dw", 16)

    def emit_oproj(q5, wo_dw):
        aov_now = P.cnt("aov")
        dx_now = P.cnt("dx")
        s1v_now = P.cnt("s1v")
        base_orv = P.cnt("orv")
        for qt in range(4):
            yb = qt % 2
            first_w = [("aov", aov_now), ("dx", dx_now), ("dw", max(wo_dw, P.cnt("dw"))),
                       ("exa", P.cnt("exa")), ("s1v", s1v_now),
                       ("orv", base_orv + qt - 1 if qt >= 2 else max(0, base_orv - 1))]
            for half in range(2):
                ysl = B.ps_y[:, yb * 1024 + half * 512: yb * 1024 + (half + 1) * 512]
                P.op("tensor", lambda e, ysl=ysl, half=half: e.matmul(
                    ysl, B.oK1[:], B.brow[0:1, half * 512:(half + 1) * 512],
                    start=True, stop=False), first_w)
                first_w = []
                for prr in range(8):
                    P.op("tensor", lambda e, ysl=ysl, prr=prr, q5=q5, qt=qt, half=half: e.matmul(
                        ysl, B.apair[:, prr, qt * 128:(qt + 1) * 128],
                        B.wslot3[:, prr, half * 512:(half + 1) * 512],
                        start=False, stop=(prr == 7)),
                        (), "opp" if (prr == 7 and half == 1) else None)
            tt = q5 * 4 + qt
            P.op("vector", lambda e, tt=tt, yb=yb: e.tensor_tensor(
                B.A_x[:, tt * 1024:(tt + 1) * 1024], B.A_x[:, tt * 1024:(tt + 1) * 1024],
                B.ps_y[:, yb * 1024:(yb + 1) * 1024], ALU.add),
                [("opp", P.cnt("opp"))], "orv")

    emit_init()
    ydma = 0
    for s in range(NSEQ):
        xin = P.op("sync", lambda e, s=s: e.dma_start(
            B.A_x[:].rearrange("p (t u) -> p t u", u=1024),
            B.x_d[s].rearrange("(t p) u -> p t u", p=128)),
            [("dx", ydma)], "dx", 16)
        for call in range(2):
            cdw = emit_ffn_consts()
            emit_ln(s, True, first_waits=[("dx", xin)] if call == 0 else [], const_dw=cdw)
            barrier()
            emit_ffn(s, call, cdw)
            barrier()
            if call == 0:
                emit_ln(s, False)
                barrier()
                wo_dw = emit_qkv(s)
                barrier()
                emit_attn(s, wo_dw)
                barrier()
        barrier()
        ydma = P.op("sync", lambda e, s=s: e.dma_start(
            B.y_d[s].rearrange("(t p) u -> p t u", p=128),
            B.A_x[:].rearrange("p (t u) -> p t u", u=1024)), (), "dx", 16)
    barrier()
    fin = P.cnt("dx")
    for eng in ENGS:
        P.ops[eng].append(("WAIT_ONLY", (("dx", fin),), None, 0))

    simulate_schedule(P, barrier_marks)
    P.replay()
    return nc


def kernel(**inputs):
    gi = lambda n: np.ascontiguousarray(np.asarray(inputs[n], dtype=np.float32))
    x = gi("L0_attn_x")
    W1 = gi("L0_ffn_W1"); W2 = gi("L0_ffn_W2")
    Wg = gi("L0_ffn_Wg"); bg = gi("L0_ffn_bg")
    b1 = gi("L0_ffn_b1"); b2 = gi("L0_ffn_b2")
    Wq = gi("L0_attn_Wq"); Wk = gi("L0_attn_Wk"); Wv = gi("L0_attn_Wv"); Wo = gi("L0_attn_Wo")
    bq = gi("L0_attn_bq"); bk = gi("L0_attn_bk"); bv = gi("L0_attn_bv"); bo = gi("L0_attn_bo")
    nqw = gi("L0_attn_norm_q_w"); nkw = gi("L0_attn_norm_k_w")

    w1r = np.ascontiguousarray(W1.reshape(8, 8, 128, 32, 128).transpose(2, 0, 3, 1, 4)
                               .reshape(128, NHC, NDC, 128))
    w2r = np.ascontiguousarray(W2.reshape(8, 32, 128, 1024).transpose(2, 0, 1, 3)
                               .reshape(128, NHC, 1024))
    wqt = np.ascontiguousarray(Wq.reshape(8, 128, 8, 128).transpose(3, 2, 0, 1))
    wkt = np.ascontiguousarray(Wk.reshape(8, 128, 8, 128).transpose(3, 2, 0, 1))
    wvt = np.ascontiguousarray(Wv.reshape(1024, 8, 128).transpose(2, 1, 0))
    wot = np.ascontiguousarray(Wo.reshape(1024, 8, 128).transpose(2, 1, 0))
    wgr = np.ascontiguousarray(Wg.reshape(8, 128, 8).transpose(1, 0, 2))
    b1c = np.ascontiguousarray(b1.reshape(8, 32, 128).transpose(2, 0, 1).reshape(128, NHC))
    nqwr = np.ascontiguousarray(nqw.reshape(8, 128).T)
    nkwr = np.ascontiguousarray(nkw.reshape(8, 128).T)

    base = {
        "w1r": w1r, "w2r": w2r, "wqt": wqt, "wkt": wkt, "wvt": wvt, "wot": wot,
        "wgr": wgr, "b1c": b1c, "b2": b2, "bg": bg.reshape(1, 8),
        "bq": bq.reshape(1, D), "bk": bk.reshape(1, D), "bv": bv.reshape(1, D),
        "bo": bo.reshape(1, D), "nqw": nqwr, "nkw": nkwr,
    }
    nc = build_nc()
    in_maps = []
    for c in range(8):
        m = dict(base)
        m["x"] = np.ascontiguousarray(x[c * NSEQ:(c + 1) * NSEQ])
        in_maps.append(m)
    res = run_bass_kernel_spmd(nc, in_maps, core_ids=list(range(8)))
    out = np.concatenate([np.asarray(r["y"], dtype=np.float32).reshape(NSEQ, T, D)
                          for r in res.results], axis=0)
    return out
